# version 24
# baseline (speedup 1.0000x reference)
import contextlib
import numpy as np

import concourse.bass as bass
import concourse.mybir as mybir
from concourse.bass_utils import run_bass_kernel_spmd

F32 = mybir.dt.float32
BF16 = mybir.dt.bfloat16
I8 = mybir.dt.int8
AF = mybir.ActivationFunctionType
ALU = mybir.AluOpType
AX = mybir.AxisListType.X

ALL8 = [[0, 1, 2, 3, 4, 5, 6, 7]]

MAGIC = 1.5 * 2 ** 23
EPS_LN = 1e-5
EPS_Q = 1e-5
EXP_BIAS = -25.0
MASKV = -30000.0

E = 1024
NPOS = 2048
NT = 16
NOT = 16
OWN = 2048

ENGS = ("sync", "gpsimd", "vector", "scalar", "tensor")


def build():
    nc = bass.Bass("TRN2", target_bir_lowering=False)

    xpos = nc.dram_tensor("xpos", [NPOS, E], F32, kind="ExternalInput").ap()
    xres = nc.dram_tensor("xres", [OWN, E], F32, kind="ExternalInput").ap()
    wqT = nc.dram_tensor("wqT", [E, E], F32, kind="ExternalInput").ap()
    wkT = nc.dram_tensor("wkT", [E, E], F32, kind="ExternalInput").ap()
    wvT = nc.dram_tensor("wvT", [E, E], F32, kind="ExternalInput").ap()
    woT = nc.dram_tensor("woT", [E, E], F32, kind="ExternalInput").ap()
    w8T = nc.dram_tensor("w8T", [E, E], I8, kind="ExternalInput").ap()
    bqr = nc.dram_tensor("bqr", [1, E], F32, kind="ExternalInput").ap()
    bkr = nc.dram_tensor("bkr", [1, E], F32, kind="ExternalInput").ap()
    bvr = nc.dram_tensor("bvr", [1, E], F32, kind="ExternalInput").ap()
    bor = nc.dram_tensor("bor", [1, E], F32, kind="ExternalInput").ap()
    fscr = nc.dram_tensor("fscr", [1, E], F32, kind="ExternalInput").ap()
    fbr = nc.dram_tensor("fbr", [1, E], F32, kind="ExternalInput").ap()
    idfd = nc.dram_tensor("idfd", [128, 128], F32, kind="ExternalInput").ap()
    maskd = nc.dram_tensor("maskd", [128, 128], F32, kind="ExternalInput").ap()
    y = nc.dram_tensor("y", [OWN, E], F32, kind="ExternalOutput").ap()

    r1d = nc.dram_tensor("r1d", [OWN, E], F32, kind="Internal").ap()
    arin = nc.dram_tensor("arin", [1, 16], F32, kind="Internal").ap()
    arout = nc.dram_tensor("arout", [1, 16], F32, kind="Internal").ap()

    es = contextlib.ExitStack()

    def mk(stack, name, shape, dt, side=None):
        if side is None:
            return stack.enter_context(nc.sbuf_tensor(name, shape, dt))
        return stack.enter_context(nc.sbuf_tensor(name, shape, dt, side=side))

    ps = es.enter_context(nc.psum_tensor("ps", [128, 2048], F32))
    pu = es.enter_context(nc.psum_tensor("pu", [128, 1024], F32))
    pt = es.enter_context(nc.psum_tensor("pt", [128, 2048], BF16))
    pb = ps[:, 1536:2048]

    idb = mk(es, "idb", [128, 128], BF16)
    idf = mk(es, "idf", [128, 128], F32)
    maskb = mk(es, "maskb", [128, 128], BF16)
    onesb = mk(es, "onesb", [1, 512], BF16)
    onesf = mk(es, "onesf", [1, 128], F32)
    bneg = mk(es, "bneg", [128, 1], F32)
    bql = mk(es, "bql", [1, E], BF16)
    bkl = mk(es, "bkl", [1, E], BF16)
    bvl = mk(es, "bvl", [1, E], BF16)
    bol = mk(es, "bol", [1, E], BF16)
    m_all = mk(es, "m_all", [128, NT], F32)
    var_all = mk(es, "var_all", [128, NT], F32)
    lnv = mk(es, "lnv", [128, NT], F32)
    rstd_all = mk(es, "rstd_all", [128, NT], F32)
    ssqa = mk(es, "ssqa", [128, NT], F32)
    e2a = mk(es, "e2a", [128, NT], F32)
    msqa = mk(es, "msqa", [128, NT], F32)
    m2 = mk(es, "m2", [128, NOT], F32)
    var2 = mk(es, "var2", [128, NOT], F32)
    lnv2 = mk(es, "lnv2", [128, NOT], F32)
    rstd2 = mk(es, "rstd2", [128, NOT], F32)
    cmax = mk(es, "cmax", [128, NOT], F32)
    rsxs = mk(es, "rsxs", [128, NOT], F32)
    nm2 = mk(es, "nm2", [128, NOT], F32)
    am16 = mk(es, "am16", [128, NOT], F32)
    am_row = mk(es, "am_row", [1, 16], F32)
    xs_sb = mk(es, "xs_sb", [1, 16], F32)
    xsv = mk(es, "xsv", [1, 1], F32)
    xsinv = mk(es, "xsinv", [1, 1], F32)
    xs_col = mk(es, "xs_col", [128, 1], F32)
    amcol = mk(es, "amcol", [128, 1], F32)
    sums = mk(es, "sums", [128, 16], F32)
    recs = mk(es, "recs", [128, 16], F32)

    SNAMES = ("sL", "sG", "sCC", "sP", "sA", "sV", "sY")
    S = {n: es.enter_context(nc.semaphore(n)) for n in SNAMES}
    C = {n: 0 for n in SNAMES}

    block = es.enter_context(nc.Block())
    P = {k: [] for k in ENGS}

    def wait(eng, sem, v):
        if v is not None and v > 0:
            P[eng].append(lambda e, s=S[sem], vv=v: e.wait_ge(s, vv))

    def step(eng, f, sem=None, n=1):
        if sem is None:
            P[eng].append(f)
            return None
        C[sem] += n
        v = C[sem]
        P[eng].append(lambda e, ff=f, s=S[sem], nn=n: ff(e).then_inc(s, nn))
        return v

    ev_idb = step("gpsimd", lambda e: e.dma_start(out=idb[:, :], in_=idfd[:, :]), "sG", 16)
    ev_idf = step("gpsimd", lambda e: e.dma_start(out=idf[:, :], in_=idfd[:, :]), "sG", 16)
    ev_msk = step("gpsimd", lambda e: e.dma_start(out=maskb[:, :], in_=maskd[:, :]), "sG", 16)
    step("gpsimd", lambda e: e.dma_start(out=bql[:, :], in_=bqr[:, :]), "sG", 16)
    step("gpsimd", lambda e: e.dma_start(out=bkl[:, :], in_=bkr[:, :]), "sG", 16)
    ev_bv = step("gpsimd", lambda e: e.dma_start(out=bvl[:, :], in_=bvr[:, :]), "sG", 16)
    ev_bo2 = step("gpsimd", lambda e: e.dma_start(out=bol[:, :], in_=bor[:, :]), "sG", 16)

    step("vector", lambda e: e.memset(onesb[:, :], 1.0))
    step("vector", lambda e: e.memset(onesf[:, :], 1.0))
    step("vector", lambda e: e.memset(am_row[:, :], 0.0))
    ev_mset = step("vector", lambda e: e.memset(bneg[:, :], EXP_BIAS), "sV")

    Zfm = mk(es, "Zfm", [128, 8 * OWN], BF16, side="right")
    Vh = mk(es, "Vh", [128, 16384], BF16, side="right")
    wbuf = mk(es, "wbuf", [128, 8192], BF16, side="right")
    ln_fm = mk(es, "ln_fm", [128, 8 * NPOS], BF16, side="right")
    stXT = contextlib.ExitStack()
    xt = mk(stXT, "xt", [128, 2048], F32, side="right")
    ln_tmp = mk(stXT, "ln_tmp", [128, 2048], BF16, side="right")

    ev_x1 = {}
    ev_app = {}
    ev_lnfm = {}
    ev_tr = {}
    ev_stA = {}
    wait("tensor", "sG", ev_idb)
    for i in range(NT):
        sl = 1024 * (i % 2)
        if i >= 2:
            wait("sync", "sV", ev_app[i - 2])
        ev_x1[i] = step("sync", lambda e, i=i, sl=sl: e.dma_start(
            out=xt[:, sl:sl + 1024],
            in_=xpos[128 * i:128 * (i + 1), :]), "sL", 16)
        wait("scalar", "sL", ev_x1[i])
        if i >= 2:
            wait("scalar", "sP", ev_tr[i - 2])
        step("scalar", lambda e, i=i, sl=sl: e.activation(
            out=ln_tmp[:, sl:sl + 1024], in_=xt[:, sl:sl + 1024],
            func=AF.Identity, accum_out=m_all[:, i:i + 1]))
        ev_stA[i] = step("scalar", lambda e, i=i, sl=sl: e.activation(
            out=ln_tmp[:, sl:sl + 1024], in_=xt[:, sl:sl + 1024],
            func=AF.Square, accum_out=ssqa[:, i:i + 1]), "sA")
        wait("vector", "sA", ev_stA[i])
        step("vector", lambda e, i=i: e.tensor_scalar(
            out=m_all[:, i:i + 1], in0=m_all[:, i:i + 1], scalar1=1.0 / E,
            scalar2=None, op0=ALU.mult))
        step("vector", lambda e, i=i: e.tensor_scalar(
            out=e2a[:, i:i + 1], in0=ssqa[:, i:i + 1], scalar1=1.0 / E,
            scalar2=None, op0=ALU.mult))
        step("vector", lambda e: e.drain())
        step("vector", lambda e, i=i: e.tensor_tensor(
            out=msqa[:, i:i + 1], in0=m_all[:, i:i + 1], in1=m_all[:, i:i + 1],
            op=ALU.mult))
        step("vector", lambda e: e.drain())
        step("vector", lambda e, i=i: e.tensor_tensor(
            out=var_all[:, i:i + 1], in0=e2a[:, i:i + 1], in1=msqa[:, i:i + 1],
            op=ALU.subtract))
        step("vector", lambda e: e.drain())
        ev_vcol = step("vector", lambda e, i=i: e.tensor_scalar(
            out=var_all[:, i:i + 1], in0=var_all[:, i:i + 1], scalar1=EPS_LN,
            scalar2=None, op0=ALU.add), "sV")
        wait("scalar", "sV", ev_vcol)
        step("scalar", lambda e, i=i: e.activation(
            out=lnv[:, i:i + 1], in_=var_all[:, i:i + 1], func=AF.Ln))
        step("scalar", lambda e: e.drain())
        ev_rstd_t = step("scalar", lambda e, i=i: e.activation(
            out=rstd_all[:, i:i + 1], in_=lnv[:, i:i + 1], func=AF.Exp,
            scale=-0.5), "sA")
        wait("vector", "sA", ev_rstd_t)
        ev_app[i] = step("vector", lambda e, i=i, sl=sl: e.tensor_scalar(
            out=ln_tmp[:, sl:sl + 1024],
            in0=xt[:, sl:sl + 1024],
            scalar1=m_all[:, i:i + 1], scalar2=rstd_all[:, i:i + 1],
            op0=ALU.subtract, op1=ALU.mult), "sV")
        wait("tensor", "sV", ev_app[i])
        if i >= 1:
            wait("tensor", "sV", ev_lnfm[i - 1])
        for j in range(8):
            def trj(e, i=i, j=j, sl=sl):
                return e.transpose(
                    out=pt[:, 128 * j:128 * (j + 1)],
                    in_=ln_tmp[:, sl + 128 * j:sl + 128 * (j + 1)],
                    identity=idb[:, :])
            if j < 7:
                step("tensor", trj)
            else:
                ev_tr[i] = step("tensor", trj, "sP")
        wait("vector", "sP", ev_tr[i])
        dst = ln_fm[:, :].rearrange("p (j t) -> p j t", j=8)[:, :, 128 * i:128 * (i + 1)]
        ev_lnfm[i] = step("vector", lambda e, dst=dst: e.tensor_copy(
            out=dst, in_=pt[:, 0:1024]), "sV")

    stXT.close()
    stVd = contextlib.ExitStack()
    Vd = mk(stVd, "Vd", [128, 16384], BF16, side="right")

    pslot_ev = {0: None, 1: None}
    gidx = 0

    def ln_ready_ev(tb):
        return ev_lnfm[min(4 * tb + 3, NT - 1)]

    def emit_proj(bias_row, dst, ev_w_loaded):
        nonlocal gidx
        last_mm = None
        last_dve = None
        last_act = None
        for fc in range(8):
            for tb in range(4):
                slot = gidx % 2
                gidx += 1
                base = 1024 * slot
                wait("tensor", "sG", ev_w_loaded)
                wait("tensor", "sV", ln_ready_ev(tb))
                pe_ev = pslot_ev[slot]
                if pe_ev is not None:
                    wait("tensor", "sV", pe_ev[0])
                    wait("tensor", "sA", pe_ev[1])
                for j in range(8):
                    def mmj(e, j=j, fc=fc, tb=tb, base=base):
                        return e.matmul(
                            out=ps[:, base:base + 512],
                            lhsT=wbuf[:, 1024 * j + 128 * fc:1024 * j + 128 * (fc + 1)],
                            rhs=ln_fm[:, 2048 * j + 512 * tb:2048 * j + 512 * (tb + 1)],
                            start=(j == 0), stop=False)
                    step("tensor", mmj)
                def mmb(e, fc=fc, base=base, br=bias_row):
                    return e.matmul(
                        out=ps[:, base:base + 512],
                        lhsT=br[0:1, 128 * fc:128 * (fc + 1)],
                        rhs=onesb[0:1, 0:512],
                        start=False, stop=True)
                last_mm = step("tensor", mmb, "sP")
                wait("vector", "sP", last_mm)
                wait("scalar", "sP", last_mm)
                for k, (i_loc, gp) in enumerate(((0, 0), (0, 1), (1, 0), (1, 1))):
                    i_head = 2 * fc + i_loc
                    src = ps[:, base:base + 512].rearrange(
                        "p (mm q) -> p mm q", mm=32)[
                        64 * i_loc:64 * i_loc + 64, :, 8 * gp:8 * gp + 8]
                    dstv = dst[:, :].rearrange(
                        "p (mm i tl) -> p mm i tl", i=16, tl=8)[
                        64 * gp:64 * gp + 64, 32 * tb:32 * tb + 32,
                        i_head:i_head + 1, :]
                    if k < 2:
                        last_dve = step("vector", lambda e, dstv=dstv, src=src:
                                        e.tensor_copy(out=dstv, in_=src), "sV")
                    else:
                        last_act = step("scalar", lambda e, dstv=dstv, src=src:
                                        e.activation(out=dstv, in_=src,
                                                     func=AF.Identity), "sA")
                pslot_ev[slot] = (last_dve, last_act)
        return last_mm, last_dve, last_act

    ev_wv = None
    for j in range(8):
        ev_wv = step("gpsimd", lambda e, j=j: e.dma_start(
            out=wbuf[:, 1024 * j:1024 * (j + 1)], in_=wvT[128 * j:128 * (j + 1), :]), "sG", 16)
    wait("gpsimd", "sG", ev_bv)
    ev_vmm, ev_vdve, ev_vact = emit_proj(bvl, Vd, ev_wv)

    ev_vh_last = None
    ev_vh_tr_last = None
    prev_vh_evac = None
    for g4 in range(64):
        wait("tensor", "sV", ev_vdve)
        wait("tensor", "sA", ev_vact)
        if g4 == 0:
            wait("tensor", "sV", ev_lnfm[NT - 1])
        elif prev_vh_evac is not None:
            wait("tensor", "sA", prev_vh_evac)
        mmt = None
        for k in range(4):
            g = 4 * g4 + k
            m, gp = g // 2, g % 2
            def trv(e, m=m, gp=gp, k=k, g4=g4):
                return e.transpose(
                    out=pt[:, 1024 * gp + 512 * (g4 % 2) + 64 * (k // 2):
                           1024 * gp + 512 * (g4 % 2) + 64 * (k // 2) + 64],
                    in_=Vd[64 * gp:64 * gp + 64, 128 * m:128 * (m + 1)],
                    identity=idb[64 * gp:64 * gp + 64, 64 * gp:64 * gp + 64])
            if k < 3:
                step("tensor", trv)
            else:
                mmt = step("tensor", trv, "sP")
        ev_vh_tr_last = mmt
        wait("scalar", "sP", mmt)
        src_ap = pt[:, :].rearrange("p (gp q mg c) -> p mg gp q c", gp=2, q=2, mg=8)[
            :, 0:2, :, g4 % 2:g4 % 2 + 1, :]
        prev_vh_evac = step("scalar", lambda e, g4=g4, src_ap=src_ap: e.activation(
            out=Vh[:, :].rearrange("p (g4x mm gp c) -> p g4x mm gp c",
                                   mm=2, gp=2, c=64)[:, g4:g4 + 1, :, :, :],
            in_=src_ap, func=AF.Identity), "sA")
        ev_vh_last = prev_vh_evac

    stVd.close()
    stQK = contextlib.ExitStack()
    Qd = mk(stQK, "Qd", [128, 16384], BF16, side="right")
    Kd = mk(stQK, "Kd", [128, 16384], BF16, side="right")

    wait("gpsimd", "sP", ev_vmm)
    ev_wq = None
    for j in range(8):
        ev_wq = step("gpsimd", lambda e, j=j: e.dma_start(
            out=wbuf[:, 1024 * j:1024 * (j + 1)], in_=wqT[128 * j:128 * (j + 1), :]), "sG", 16)
    ev_qmm, _, _ = emit_proj(bql, Qd, ev_wq)

    wait("gpsimd", "sP", ev_qmm)
    ev_wk = None
    for j in range(8):
        ev_wk = step("gpsimd", lambda e, j=j: e.dma_start(
            out=wbuf[:, 1024 * j:1024 * (j + 1)], in_=wkT[128 * j:128 * (j + 1), :]), "sG", 16)
    ev_kmm, ev_kdve, ev_kact = emit_proj(bkl, Kd, ev_wk)

    expA = mk(stQK, "expA", [128, 2048], BF16, side="right")
    expAT = mk(stQK, "expAT", [128, 2048], BF16, side="right")

    wait("tensor", "sV", ev_kdve)
    wait("tensor", "sA", ev_kact)
    wait("tensor", "sA", ev_vh_last)
    wait("tensor", "sG", ev_msk)
    wait("scalar", "sV", ev_mset)

    ev_exp = {}
    ev_trx = {}
    ev_atx = {}
    ev_avout = {}
    ev_zc = {}
    ev_ts = {}
    zcv_ev = {}
    zca_ev = {}
    ev_smaskd = {}

    def pt_slot(m):
        s4 = m % 4
        return 1024 * (s4 % 2) + 512 * (s4 // 2)

    def emit_S(m):
        if m >= 2:
            wait("tensor", "sA", ev_exp[(m - 2, 3)])
        for k in range(4):
            mm = 2 * m + k // 2
            gp = k % 2
            col = 1024 * (m % 2) + 512 * gp + 128 * (k // 2)
            def mmS(e, mm=mm, gp=gp, col=col, k=k):
                return e.matmul(
                    out=ps[:, col:col + 128],
                    lhsT=Qd[64 * gp:64 * gp + 64, 128 * mm:128 * (mm + 1)],
                    rhs=Kd[64 * gp:64 * gp + 64, 128 * mm:128 * (mm + 1)],
                    start=(k < 2), stop=False)
            step("tensor", mmS)
        evm = None
        for k in range(4):
            gp = k % 2
            col = 1024 * (m % 2) + 512 * gp + 128 * (k // 2)
            def mmM(e, col=col):
                return e.matmul(
                    out=ps[:, col:col + 128],
                    lhsT=maskb[:, :], rhs=idb[:, :],
                    start=False, stop=True)
            evm = step("tensor", mmM, "sP")
        ev_smaskd[m] = evm

    def emit_exp(m):
        q4 = m % 4
        wait("scalar", "sP", ev_smaskd[m])
        if m >= 4:
            wait("scalar", "sP", ev_trx[m - 4])
        for k in range(4):
            gp = k % 2
            col = 1024 * (m % 2) + 512 * gp + 128 * (k // 2)
            def expf(e, col=col, q4=q4, k=k):
                return e.activation(
                    out=expA[:, 512 * q4 + 128 * k:512 * q4 + 128 * (k + 1)],
                    in_=ps[:, col:col + 128],
                    func=AF.Exp, scale=0.125, bias=bneg[:, 0:1],
                    accum_out=sums[:, 4 * q4 + k:4 * q4 + k + 1])
            ev_exp[(m, k)] = step("scalar", expf, "sA")

    def emit_norm(m):
        q4 = m % 4
        wait("vector", "sA", ev_exp[(m, 3)])
        step("vector", lambda e, q4=q4: e.reciprocal(
            out=recs[:, 4 * q4:4 * q4 + 4], in_=sums[:, 4 * q4:4 * q4 + 4]))
        step("vector", lambda e: e.drain())
        ev = None
        for k in range(4):
            ev = step("vector", lambda e, q4=q4, k=k: e.tensor_scalar(
                out=expA[:, 512 * q4 + 128 * k:512 * q4 + 128 * (k + 1)],
                in0=expA[:, 512 * q4 + 128 * k:512 * q4 + 128 * (k + 1)],
                scalar1=recs[:, 4 * q4 + k:4 * q4 + k + 1], scalar2=None,
                op0=ALU.mult), "sV")
        ev_ts[m] = ev

    def emit_trx(m):
        q4 = m % 4
        base = pt_slot(m)
        wait("tensor", "sV", ev_ts[m])
        if m >= 2:
            wait("tensor", "sA", ev_atx[m - 2])
        for k in range(4):
            def trx(e, base=base, q4=q4, k=k):
                return e.transpose(
                    out=pt[:, base + 128 * k:base + 128 * (k + 1)],
                    in_=expA[:, 512 * q4 + 128 * k:512 * q4 + 128 * (k + 1)],
                    identity=idb[:, :])
            if k < 3:
                step("tensor", trx)
            else:
                ev_trx[m] = step("tensor", trx, "sP")

    def emit_atx(m):
        q4 = m % 4
        base = pt_slot(m)
        wait("scalar", "sP", ev_trx[m])
        if m >= 4:
            wait("scalar", "sP", ev_avout[m - 4])
        ev_atx[m] = step("scalar", lambda e, base=base, q4=q4: e.activation(
            out=expAT[:, 512 * q4:512 * (q4 + 1)],
            in_=pt[:, base:base + 512],
            func=AF.Identity), "sA")

    def emit_av(m):
        q4 = m % 4
        wait("tensor", "sA", ev_atx[m])
        if m >= 2:
            wait("tensor", "sV", zcv_ev[m - 2])
            wait("tensor", "sA", zca_ev[m - 2])
        mmav = None
        for k in range(4):
            mm = 2 * m + k // 2
            gg = 2 * mm + (k % 2)
            def mmv(e, gg=gg, q4=q4, k=k, m=m):
                return e.matmul(
                    out=pu[0:64, 512 * (m % 2) + 128 * k:512 * (m % 2) + 128 * (k + 1)],
                    lhsT=Vh[:, 64 * gg:64 * (gg + 1)],
                    rhs=expAT[:, 512 * q4 + 128 * k:512 * q4 + 128 * (k + 1)],
                    start=True, stop=True)
            if k < 3:
                step("tensor", mmv)
            else:
                mmav = step("tensor", mmv, "sP")
        ev_avout[m] = mmav

    def zc_ap(m, pi, par):
        src = pu[:, 512 * (m % 2) + 256 * pi:512 * (m % 2) + 256 * pi + 256]\
            .rearrange("p (kg i u2 pr) -> p kg u2 pr i", kg=2, i=16, u2=4)[
            0:64, :, :, par:par + 1, :]
        dst = Zfm[:, :].rearrange(
            "p (kg u2 i a) -> p kg u2 i a", kg=2, u2=4, i=16)[
            64 * par:64 * par + 64, :, :, :, 2 * m + pi:2 * m + pi + 1]
        return src, dst

    def emit_zcv(m):
        wait("vector", "sP", ev_avout[m])
        ev = None
        for pi, par in ((0, 0), (0, 1), (1, 0)):
            src, dst = zc_ap(m, pi, par)
            ev = step("vector", lambda e, dst=dst, src=src: e.tensor_copy(
                out=dst, in_=src), "sV")
        zcv_ev[m] = ev

    def emit_zca(m):
        wait("scalar", "sP", ev_avout[m])
        wait("scalar", "sV", zcv_ev[m])
        src, dst = zc_ap(m, 1, 1)
        zca_ev[m] = step("scalar", lambda e, dst=dst, src=src: e.activation(
            out=dst, in_=src, func=AF.Identity), "sA")

    for t in range(67):
        if t <= 63:
            emit_S(t)
        if 1 <= t <= 64:
            emit_exp(t - 1)
            emit_norm(t - 1)
            emit_trx(t - 1)
        if 2 <= t <= 65:
            emit_atx(t - 2)
            emit_av(t - 2)
            emit_zcv(t - 2)
        if 3 <= t <= 66:
            emit_zca(t - 3)

    ev_explast = ev_exp[(63, 3)]
    ev_avlast = ev_avout[63]
    ev_zlast = (zcv_ev[63], zca_ev[63])
    ev_atxlast = ev_atx[63]

    stQK.close()
    stO = contextlib.ExitStack()
    woT_sb = mk(stO, "woT_sb", [128, 8192], BF16, side="right")
    xrt = mk(stO, "xrt", [128, 2048], F32, side="right")
    r1w = mk(stO, "r1w", [128, 2048], F32, side="right")
    sqw = mk(stO, "sqw", [128, 1024], F32, side="right")
    bfb = mk(es, "bfb", [128, E], F32)
    bwsx = mk(es, "bwsx", [128, E], F32)
    fsc_sb = mk(es, "fsc_sb", [1, E], F32)
    fb_sb = mk(es, "fb_sb", [1, E], F32)
    wsxs = mk(es, "wsxs", [1, E], F32)
    wait("gpsimd", "sP", ev_avlast)
    ev_fsc = step("gpsimd", lambda e: e.dma_start(out=fsc_sb[:, :], in_=fscr[:, :]), "sG", 16)
    ev_fb = step("gpsimd", lambda e: e.dma_start(out=fb_sb[:, :], in_=fbr[:, :]), "sG", 16)
    ev_wo = None
    for j in range(8):
        ev_wo = step("gpsimd", lambda e, j=j: e.dma_start(
            out=woT_sb[:, 1024 * j:1024 * (j + 1)], in_=woT[128 * j:128 * (j + 1), :]), "sG", 16)

    ev_pb_free = None

    def emit_bcast_row(src_row, dst_fm, ev_src_g, ev_src_v=None):
        nonlocal ev_pb_free
        out_ev = None
        for h in range(2):
            wait("tensor", "sG", ev_src_g)
            wait("tensor", "sV", ev_src_v)
            if ev_pb_free is not None:
                wait("tensor", "sV", ev_pb_free)
            def mmh(e, h=h, src_row=src_row):
                return e.matmul(out=pb[:, :], lhsT=onesf[0:1, 0:128],
                                rhs=src_row[0:1, 512 * h:512 * (h + 1)],
                                start=True, stop=True)
            mm = step("tensor", mmh, "sP")
            wait("vector", "sP", mm)
            ev_pb_free = step("vector", lambda e, h=h, dst_fm=dst_fm: e.tensor_copy(
                out=dst_fm[:, 512 * h:512 * (h + 1)], in_=pb[:, :]), "sV")
            out_ev = ev_pb_free
        return out_ev

    wait("tensor", "sA", ev_explast)
    emit_bcast_row(fb_sb, bfb, ev_fb)

    ev_p1 = {}
    ev_ttl = {}
    ev_r1dma = {}
    oslot_ev = {(pz, h): None for pz in range(2) for h in range(2)}

    def o_ap(sb, half):
        if sb % 2 == 0:
            return ps[:, 1024 * half:1024 * half + 512]
        return pu[:, 512 * half:512 * half + 512]

    wait("sync", "sP", ev_avlast)
    for sb in range(16):
        sl = 1024 * (sb % 2)
        pz = sb % 2
        if sb >= 2:
            wait("sync", "sV", ev_ttl[sb - 2])
        ev_xr = step("sync", lambda e, sb=sb, sl=sl: e.dma_start(
            out=xrt[:, sl:sl + 1024], in_=xres[128 * sb:128 * (sb + 1), :]), "sL", 16)
        lastmm = {}
        for half in range(2):
            wait("tensor", "sG", ev_wo)
            wait("tensor", "sV", ev_zlast[0])
            wait("tensor", "sA", ev_zlast[1])
            if sb == 0 and half == 0:
                wait("tensor", "sA", ev_explast)
            if oslot_ev[(pz, half)] is not None:
                wait("tensor", "sV", oslot_ev[(pz, half)])
            if sb == 0 and half == 0:
                wait("tensor", "sG", ev_bo2)
            for c in range(8):
                def mmo(e, c=c, sb=sb, half=half):
                    return e.matmul(
                        out=o_ap(sb, half),
                        lhsT=Zfm[:, 2048 * c + 128 * sb:2048 * c + 128 * (sb + 1)],
                        rhs=woT_sb[:, 1024 * c + 512 * half:1024 * c + 512 * (half + 1)],
                        start=(c == 0), stop=False)
                step("tensor", mmo)
            def mmob(e, sb=sb, half=half):
                return e.matmul(
                    out=o_ap(sb, half),
                    lhsT=onesb[0:1, 0:128],
                    rhs=bol[0:1, 512 * half:512 * (half + 1)],
                    start=False, stop=True)
            lastmm[half] = step("tensor", mmob, "sP")
        wait("vector", "sP", lastmm[1])
        wait("vector", "sL", ev_xr)
        if sb >= 2:
            wait("vector", "sG", ev_r1dma[sb - 2])
        for half in range(2):
            oslot_ev[(pz, half)] = step("vector", lambda e, sb=sb, half=half, sl=sl: e.tensor_tensor(
                out=r1w[:, sl + 512 * half:sl + 512 * (half + 1)],
                in0=o_ap(sb, half),
                in1=xrt[:, sl + 512 * half:sl + 512 * (half + 1)], op=ALU.add), "sV")
        ev_ttl[sb] = oslot_ev[(pz, 1)]
        wait("scalar", "sV", ev_ttl[sb])
        if sb >= 1:
            wait("scalar", "sV", ev_p1[sb - 1])
        step("scalar", lambda e, sb=sb, sl=sl: e.activation(
            out=sqw[:, :], in_=r1w[:, sl:sl + 1024],
            func=AF.Identity, accum_out=m2[:, sb:sb + 1]))
        ev_sq = step("scalar", lambda e, sb=sb, sl=sl: e.activation(
            out=sqw[:, :], in_=r1w[:, sl:sl + 1024],
            func=AF.Square, accum_out=ssqa[:, sb:sb + 1]), "sA")
        wait("vector", "sA", ev_sq)
        step("vector", lambda e, sb=sb: e.tensor_scalar(
            out=m2[:, sb:sb + 1], in0=m2[:, sb:sb + 1], scalar1=1.0 / E, scalar2=None,
            op0=ALU.mult))
        step("vector", lambda e: e.drain())
        ev_nm = step("vector", lambda e, sb=sb: e.tensor_scalar(
            out=nm2[:, sb:sb + 1], in0=m2[:, sb:sb + 1], scalar1=-1.0, scalar2=None,
            op0=ALU.mult), "sV")
        wait("scalar", "sV", ev_nm)
        ev_abs = step("scalar", lambda e, sb=sb, sl=sl: e.activation(
            out=sqw[:, :], in_=r1w[:, sl:sl + 1024],
            func=AF.Abs, bias=nm2[:, sb:sb + 1]), "sA")
        wait("vector", "sA", ev_abs)
        ev_p1[sb] = step("vector", lambda e, sb=sb: e.reduce_max(
            out=cmax[:, sb:sb + 1], in_=sqw[:, :], axis=AX), "sV")
        wait("gpsimd", "sV", ev_ttl[sb])
        ev_r1dma[sb] = step("gpsimd", lambda e, sb=sb, sl=sl: e.dma_start(
            out=r1d[128 * sb:128 * (sb + 1), :], in_=r1w[:, sl:sl + 1024]), "sG", 16)

    step("vector", lambda e: e.drain())
    step("vector", lambda e: e.tensor_scalar(
        out=e2a[:, :], in0=ssqa[:, :], scalar1=1.0 / E, scalar2=None, op0=ALU.mult))
    step("vector", lambda e: e.tensor_tensor(
        out=msqa[:, :], in0=m2[:, :], in1=m2[:, :], op=ALU.mult))
    step("vector", lambda e: e.drain())
    step("vector", lambda e: e.tensor_tensor(
        out=var2[:, :], in0=e2a[:, :], in1=msqa[:, :], op=ALU.subtract))
    step("vector", lambda e: e.drain())
    ev_v2 = step("vector", lambda e: e.tensor_scalar(
        out=var2[:, :], in0=var2[:, :], scalar1=EPS_LN, scalar2=None, op0=ALU.add), "sV")
    wait("scalar", "sV", ev_v2)
    step("scalar", lambda e: e.activation(out=lnv2[:, :], in_=var2[:, :], func=AF.Ln))
    step("scalar", lambda e: e.drain())
    ev_rstd2 = step("scalar", lambda e: e.activation(
        out=rstd2[:, :], in_=lnv2[:, :], func=AF.Exp, scale=-0.5), "sA")
    wait("vector", "sA", ev_rstd2)
    step("vector", lambda e: e.tensor_tensor(
        out=am16[:, :], in0=cmax[:, :], in1=rstd2[:, :], op=ALU.mult))
    step("vector", lambda e: e.drain())
    ev_amc = step("vector", lambda e: e.reduce_max(
        out=amcol[:, :], in_=am16[:, :], axis=AX), "sV")
    wait("tensor", "sV", ev_amc)
    wait("tensor", "sG", ev_idf)
    ev_amt = step("tensor", lambda e: e.transpose(
        out=pu[0:1, 0:128], in_=amcol[:, :], identity=idf[:, :]), "sP")
    wait("vector", "sP", ev_amt)
    ev_am1 = step("vector", lambda e: e.reduce_max(
        out=am_row[0:1, 0:1], in_=pu[0:1, 0:128], axis=AX), "sV")

    wait("gpsimd", "sV", ev_am1)
    ev_arin = step("gpsimd", lambda e: e.dma_start(out=arin[:, :], in_=am_row[:, :]), "sG", 16)
    wait("gpsimd", "sG", ev_arin)
    ev_ar = step("gpsimd", lambda e: e.collective_compute(
        "AllReduce", ALU.max, replica_groups=ALL8,
        ins=[arin.opt()], outs=[arout.opt()]), "sCC", 1)
    wait("gpsimd", "sCC", ev_ar)
    ev_xs = step("gpsimd", lambda e: e.dma_start(out=xs_sb[:, :], in_=arout[:, :]), "sG", 16)

    wait("vector", "sG", ev_xs)
    step("vector", lambda e: e.tensor_scalar(
        out=xsv[:, :], in0=xs_sb[0:1, 0:1], scalar1=1.0 / 127.0, scalar2=EPS_Q,
        op0=ALU.mult, op1=ALU.max))
    step("vector", lambda e: e.drain())
    step("vector", lambda e: e.reciprocal(out=xsinv[:, :], in_=xsv[:, :]))
    step("vector", lambda e: e.drain())
    wait("vector", "sG", ev_fsc)
    ev_wsxs = step("vector", lambda e: e.tensor_scalar(
        out=wsxs[:, :], in0=fsc_sb[:, :], scalar1=xsv[0:1, 0:1], scalar2=None,
        op0=ALU.mult), "sV")
    wait("tensor", "sV", ev_wsxs)
    ev_xc = step("tensor", lambda e: e.matmul(
        out=pb[:, 0:1], lhsT=onesf[0:1, 0:128], rhs=xsinv[0:1, 0:1],
        start=True, stop=True), "sP")
    wait("vector", "sP", ev_xc)
    ev_xcc = step("vector", lambda e: e.tensor_copy(out=xs_col[:, :], in_=pb[:, 0:1]), "sV")
    ev_pb_free = ev_xcc
    ev_bwsx = emit_bcast_row(wsxs, bwsx, None, ev_src_v=ev_wsxs)
    wait("vector", "sV", ev_xcc)
    step("vector", lambda e: e.drain())
    ev_rsxs = step("vector", lambda e: e.tensor_scalar(
        out=rsxs[:, :], in0=rstd2[:, :], scalar1=xs_col[:, :], scalar2=None,
        op0=ALU.mult), "sV")

    stO.close()
    stF = contextlib.ExitStack()
    w8s = mk(stF, "w8s", [128, 8192], I8, side="right")
    wf = mk(stF, "wf", [128, 8192], BF16, side="right")
    r1t = mk(stF, "r1t", [128, 2048], F32, side="right")
    qzb = mk(stF, "qzb", [128, 1024], BF16, side="right")
    xq = mk(stF, "xq", [128, 1024], BF16, side="right")
    ysb = mk(stF, "ysb", [128, 2048], F32, side="right")
    es.enter_context(stF)

    wait("gpsimd", "sP", ev_amt)
    ev_w8 = None
    for j in range(8):
        ev_w8 = step("gpsimd", lambda e, j=j: e.dma_start(
            out=w8s[:, 1024 * j:1024 * (j + 1)], in_=w8T[128 * j:128 * (j + 1), :]), "sG", 16)

    wait("vector", "sG", ev_w8)
    wait("vector", "sG", ev_r1dma[NOT - 1])
    wait("vector", "sV", ev_rsxs)
    ev_wf = None
    for j in range(8):
        ev_wf = step("vector", lambda e, j=j: e.tensor_copy(
            out=wf[:, 1024 * j:1024 * (j + 1)], in_=w8s[:, 1024 * j:1024 * (j + 1)]), "sV")

    ev_ydma = {}
    ev_ffnev = {}
    ev_pt_free2 = None
    ffnslot_ev = {0: None, 1: None}
    ev_xq_cons = None
    for ot in range(NOT):
        sl = 1024 * (ot % 2)
        if ot >= 2:
            wait("sync", "sV", ev_ffnev[ot - 2])
        else:
            wait("sync", "sG", ev_xs)
        wait("sync", "sG", ev_r1dma[ot])
        ev_l = step("sync", lambda e, ot=ot, sl=sl: e.dma_start(
            out=r1t[:, sl:sl + 1024], in_=r1d[128 * ot:128 * (ot + 1), :]), "sL", 16)
        wait("vector", "sL", ev_l)
        if ot >= 2:
            wait("vector", "sY", ev_ydma[ot - 2])
        step("vector", lambda e, ot=ot, sl=sl: e.tensor_scalar(
            out=ysb[:, sl:sl + 1024], in0=r1t[:, sl:sl + 1024],
            scalar1=m2[:, ot:ot + 1], scalar2=rsxs[:, ot:ot + 1],
            op0=ALU.subtract, op1=ALU.mult))
        step("vector", lambda e, sl=sl: e.tensor_scalar(
            out=ysb[:, sl:sl + 1024], in0=ysb[:, sl:sl + 1024], scalar1=MAGIC, scalar2=MAGIC,
            op0=ALU.add, op1=ALU.subtract))
        ev_q = step("vector", lambda e, sl=sl: e.tensor_scalar(
            out=qzb[:, :], in0=ysb[:, sl:sl + 1024], scalar1=127.0, scalar2=-128.0,
            op0=ALU.min, op1=ALU.max), "sV")
        wait("tensor", "sV", ev_q)
        if ev_pt_free2 is None:
            wait("tensor", "sA", ev_atxlast)
        else:
            wait("tensor", "sV", ev_pt_free2)
        mmt = None
        for j in range(8):
            def trj(e, j=j):
                return e.transpose(
                    out=pt[:, 128 * j:128 * (j + 1)],
                    in_=qzb[:, 128 * j:128 * (j + 1)],
                    identity=idb[:, :])
            if j < 7:
                step("tensor", trj)
            else:
                mmt = step("tensor", trj, "sP")
        wait("vector", "sP", mmt)
        if ev_xq_cons is not None:
            wait("vector", "sP", ev_xq_cons)
        ev_xqc = step("vector", lambda e: e.tensor_copy(out=xq[:, :], in_=pt[:, 0:1024]), "sV")
        ev_pt_free2 = ev_xqc
        wait("tensor", "sV", ev_xqc)
        wait("tensor", "sV", ev_wf)
        lastmm = None
        for half in range(2):
            base = 1024 * half
            if ffnslot_ev[half] is not None:
                wait("tensor", "sV", ffnslot_ev[half])
            for j in range(8):
                def mmf(e, j=j, half=half, base=base):
                    return e.matmul(
                        out=ps[:, base:base + 512],
                        lhsT=xq[:, 128 * j:128 * (j + 1)],
                        rhs=wf[:, 1024 * j + 512 * half:1024 * j + 512 * (half + 1)],
                        start=(j == 0), stop=(j == 7))
                if j < 7:
                    step("tensor", mmf)
                else:
                    lastmm = step("tensor", mmf, "sP")
            wait("vector", "sP", lastmm)
            if half == 0:
                wait("vector", "sV", ev_bwsx)
            step("vector", lambda e, half=half, base=base, sl=sl: e.tensor_tensor(
                out=ysb[:, sl + 512 * half:sl + 512 * (half + 1)],
                in0=ps[:, base:base + 512],
                in1=bwsx[:, 512 * half:512 * (half + 1)], op=ALU.mult))
            step("vector", lambda e, half=half, sl=sl: e.tensor_tensor(
                out=ysb[:, sl + 512 * half:sl + 512 * (half + 1)],
                in0=ysb[:, sl + 512 * half:sl + 512 * (half + 1)],
                in1=bfb[:, 512 * half:512 * (half + 1)], op=ALU.add))
            ev = step("vector", lambda e, half=half, sl=sl: e.tensor_tensor(
                out=ysb[:, sl + 512 * half:sl + 512 * (half + 1)],
                in0=ysb[:, sl + 512 * half:sl + 512 * (half + 1)],
                in1=r1t[:, sl + 512 * half:sl + 512 * (half + 1)],
                op=ALU.add), "sV")
            ffnslot_ev[half] = ev
            ev_ffnev[ot] = ev
        ev_xq_cons = lastmm
        wait("gpsimd", "sV", ev_ffnev[ot])
        ev_ydma[ot] = step("gpsimd", lambda e, ot=ot, sl=sl: e.dma_start(
            out=y[128 * ot:128 * (ot + 1), :], in_=ysb[:, sl:sl + 1024]), "sY", 16)

    wait("gpsimd", "sY", ev_ydma[NOT - 1])

    @block.sync
    def _(e):
        for f in P["sync"]:
            f(e)

    @block.gpsimd
    def _(e):
        for f in P["gpsimd"]:
            f(e)

    @block.vector
    def _(e):
        for f in P["vector"]:
            f(e)

    @block.scalar
    def _(e):
        for f in P["scalar"]:
            f(e)

    @block.tensor
    def _(e):
        for f in P["tensor"]:
            f(e)

    return nc, es


def _mask_block():
    m = np.full((128, 128), MASKV, dtype=np.float32)
    r = np.arange(128)
    m[(r[:, None] % 8) == (r[None, :] % 8)] = 0.0
    return m


def kernel(L0_attn_x, L0_attn_Wq, L0_attn_bq, L0_attn_Wk, L0_attn_bk,
           L0_attn_Wv, L0_attn_bv, L0_attn_Wo, L0_attn_bo,
           L0_ffn_weight_int8, L0_ffn_weight_scale, L0_ffn_bias):
    X = np.asarray(L0_attn_x, dtype=np.float32)
    Wq = np.asarray(L0_attn_Wq, dtype=np.float32)
    Wk = np.asarray(L0_attn_Wk, dtype=np.float32)
    Wv = np.asarray(L0_attn_Wv, dtype=np.float32)
    Wo = np.asarray(L0_attn_Wo, dtype=np.float32)
    bq = np.asarray(L0_attn_bq, dtype=np.float32).reshape(1, E)
    bk = np.asarray(L0_attn_bk, dtype=np.float32).reshape(1, E)
    bv = np.asarray(L0_attn_bv, dtype=np.float32).reshape(1, E)
    bo = np.asarray(L0_attn_bo, dtype=np.float32).reshape(1, E)
    W8 = np.asarray(L0_ffn_weight_int8)
    fsc = np.asarray(L0_ffn_weight_scale, dtype=np.float32).reshape(1, E)
    fb = np.asarray(L0_ffn_bias, dtype=np.float32).reshape(1, E)

    wqTh = np.ascontiguousarray(Wq.T)
    wkTh = np.ascontiguousarray(Wk.T)
    wvTh = np.ascontiguousarray(Wv.T)
    woTh = np.ascontiguousarray(Wo.T)
    w8Th = np.ascontiguousarray(W8.T)
    ident = np.eye(128, dtype=np.float32)
    maskc = _mask_block()

    in_maps = []
    for c in range(8):
        n, hh = c // 2, c % 2
        xn = np.ascontiguousarray(X[n])
        in_maps.append({
            "xpos": np.ascontiguousarray(xn[NPOS * hh:NPOS * (hh + 1)]),
            "xres": np.ascontiguousarray(
                xn.reshape(16, 2, 128, E)[:, hh].reshape(OWN, E)),
            "wqT": wqTh, "wkT": wkTh, "wvT": wvTh, "woT": woTh, "w8T": w8Th,
            "bqr": bq, "bkr": bk, "bvr": bv, "bor": bo,
            "fscr": fsc, "fbr": fb,
            "idfd": ident, "maskd": maskc,
        })

    nc, es = build()
    try:
        res = run_bass_kernel_spmd(nc, in_maps, core_ids=list(range(8)))
    finally:
        es.close()

    out = np.empty((4, 4096, E), dtype=np.float32)
    for c in range(8):
        n, hh = c // 2, c % 2
        yv = np.asarray(res.results[c]["y"]).reshape(16, 128, E)
        out[n].reshape(16, 2, 128, E)[:, hh] = yv
    return out


# version 25 (final)
# speedup vs baseline: 1.0439x; 1.0439x over previous
import contextlib
import numpy as np

import concourse.bass as bass
import concourse.mybir as mybir
from concourse.bass_utils import run_bass_kernel_spmd

F32 = mybir.dt.float32
BF16 = mybir.dt.bfloat16
I8 = mybir.dt.int8
AF = mybir.ActivationFunctionType
ALU = mybir.AluOpType
AX = mybir.AxisListType.X

ALL8 = [[0, 1, 2, 3, 4, 5, 6, 7]]

MAGIC = 1.5 * 2 ** 23
EPS_LN = 1e-5
EPS_Q = 1e-5
EXP_BIAS = -25.0
MASKV = -30000.0

E = 1024
NPOS = 2048
NT = 16
NOT = 16
OWN = 2048

ENGS = ("sync", "gpsimd", "vector", "scalar", "tensor")


def build():
    nc = bass.Bass("TRN2", target_bir_lowering=False)

    xpos = nc.dram_tensor("xpos", [NPOS, E], F32, kind="ExternalInput").ap()
    xres = nc.dram_tensor("xres", [OWN, E], F32, kind="ExternalInput").ap()
    wqT = nc.dram_tensor("wqT", [E, E], F32, kind="ExternalInput").ap()
    wkT = nc.dram_tensor("wkT", [E, E], F32, kind="ExternalInput").ap()
    wvT = nc.dram_tensor("wvT", [E, E], F32, kind="ExternalInput").ap()
    woT = nc.dram_tensor("woT", [E, E], F32, kind="ExternalInput").ap()
    w8T = nc.dram_tensor("w8T", [E, E], I8, kind="ExternalInput").ap()
    bqr = nc.dram_tensor("bqr", [1, E], F32, kind="ExternalInput").ap()
    bkr = nc.dram_tensor("bkr", [1, E], F32, kind="ExternalInput").ap()
    bvr = nc.dram_tensor("bvr", [1, E], F32, kind="ExternalInput").ap()
    bor = nc.dram_tensor("bor", [1, E], F32, kind="ExternalInput").ap()
    fscr = nc.dram_tensor("fscr", [1, E], F32, kind="ExternalInput").ap()
    fbr = nc.dram_tensor("fbr", [1, E], F32, kind="ExternalInput").ap()
    idfd = nc.dram_tensor("idfd", [128, 128], F32, kind="ExternalInput").ap()
    maskd = nc.dram_tensor("maskd", [128, 128], F32, kind="ExternalInput").ap()
    y = nc.dram_tensor("y", [OWN, E], F32, kind="ExternalOutput").ap()

    r1d = nc.dram_tensor("r1d", [OWN, E], F32, kind="Internal").ap()
    arin = nc.dram_tensor("arin", [1, 16], F32, kind="Internal").ap()
    arout = nc.dram_tensor("arout", [1, 16], F32, kind="Internal").ap()

    es = contextlib.ExitStack()

    def mk(stack, name, shape, dt, side=None):
        if side is None:
            return stack.enter_context(nc.sbuf_tensor(name, shape, dt))
        return stack.enter_context(nc.sbuf_tensor(name, shape, dt, side=side))

    ps = es.enter_context(nc.psum_tensor("ps", [128, 2048], F32))
    pu = es.enter_context(nc.psum_tensor("pu", [128, 1024], F32))
    pt = es.enter_context(nc.psum_tensor("pt", [128, 2048], BF16))
    pb = ps[:, 1536:2048]

    idb = mk(es, "idb", [128, 128], BF16)
    idf = mk(es, "idf", [128, 128], F32)
    maskb = mk(es, "maskb", [128, 128], BF16)
    onesb = mk(es, "onesb", [1, 512], BF16)
    onesf = mk(es, "onesf", [1, 128], F32)
    bneg = mk(es, "bneg", [128, 1], F32)
    bql = mk(es, "bql", [1, E], BF16)
    bkl = mk(es, "bkl", [1, E], BF16)
    bvl = mk(es, "bvl", [1, E], BF16)
    bol = mk(es, "bol", [1, E], BF16)
    m_all = mk(es, "m_all", [128, NT], F32)
    var_all = mk(es, "var_all", [128, NT], F32)
    lnv = mk(es, "lnv", [128, NT], F32)
    rstd_all = mk(es, "rstd_all", [128, NT], F32)
    ssqa = mk(es, "ssqa", [128, NT], F32)
    e2a = mk(es, "e2a", [128, NT], F32)
    msqa = mk(es, "msqa", [128, NT], F32)
    m2 = mk(es, "m2", [128, NOT], F32)
    var2 = mk(es, "var2", [128, NOT], F32)
    lnv2 = mk(es, "lnv2", [128, NOT], F32)
    rstd2 = mk(es, "rstd2", [128, NOT], F32)
    cmax = mk(es, "cmax", [128, NOT], F32)
    rsxs = mk(es, "rsxs", [128, NOT], F32)
    nm2 = mk(es, "nm2", [128, NOT], F32)
    m2h = mk(es, "m2h", [128, 2 * NOT], F32)
    am16 = mk(es, "am16", [128, NOT], F32)
    am_row = mk(es, "am_row", [1, 16], F32)
    xs_sb = mk(es, "xs_sb", [1, 16], F32)
    xsv = mk(es, "xsv", [1, 1], F32)
    xsinv = mk(es, "xsinv", [1, 1], F32)
    xs_col = mk(es, "xs_col", [128, 1], F32)
    amcol = mk(es, "amcol", [128, 1], F32)
    sums = mk(es, "sums", [128, 16], F32)
    recs = mk(es, "recs", [128, 16], F32)

    SNAMES = ("sL", "sG", "sCC", "sP", "sA", "sV", "sY")
    S = {n: es.enter_context(nc.semaphore(n)) for n in SNAMES}
    C = {n: 0 for n in SNAMES}

    block = es.enter_context(nc.Block())
    P = {k: [] for k in ENGS}

    def wait(eng, sem, v):
        if v is not None and v > 0:
            P[eng].append(lambda e, s=S[sem], vv=v: e.wait_ge(s, vv))

    def step(eng, f, sem=None, n=1):
        if sem is None:
            P[eng].append(f)
            return None
        C[sem] += n
        v = C[sem]
        P[eng].append(lambda e, ff=f, s=S[sem], nn=n: ff(e).then_inc(s, nn))
        return v

    ev_idb = step("gpsimd", lambda e: e.dma_start(out=idb[:, :], in_=idfd[:, :]), "sG", 16)
    ev_idf = step("gpsimd", lambda e: e.dma_start(out=idf[:, :], in_=idfd[:, :]), "sG", 16)
    ev_msk = step("gpsimd", lambda e: e.dma_start(out=maskb[:, :], in_=maskd[:, :]), "sG", 16)
    step("gpsimd", lambda e: e.dma_start(out=bql[:, :], in_=bqr[:, :]), "sG", 16)
    step("gpsimd", lambda e: e.dma_start(out=bkl[:, :], in_=bkr[:, :]), "sG", 16)
    ev_bv = step("gpsimd", lambda e: e.dma_start(out=bvl[:, :], in_=bvr[:, :]), "sG", 16)
    ev_bo2 = step("gpsimd", lambda e: e.dma_start(out=bol[:, :], in_=bor[:, :]), "sG", 16)

    step("vector", lambda e: e.memset(onesb[:, :], 1.0))
    step("vector", lambda e: e.memset(onesf[:, :], 1.0))
    step("vector", lambda e: e.memset(am_row[:, :], 0.0))
    ev_mset = step("vector", lambda e: e.memset(bneg[:, :], EXP_BIAS), "sV")

    Zfm = mk(es, "Zfm", [128, 8 * OWN], BF16, side="right")
    Vh = mk(es, "Vh", [128, 16384], BF16, side="right")
    wbuf = mk(es, "wbuf", [128, 8192], BF16, side="right")
    ln_fm = mk(es, "ln_fm", [128, 8 * NPOS], BF16, side="right")
    stXT = contextlib.ExitStack()
    xt = mk(stXT, "xt", [128, 2048], F32, side="right")
    ln_tmp = mk(stXT, "ln_tmp", [128, 2048], BF16, side="right")

    ev_x1 = {}
    ev_app = {}
    ev_lnfm = {}
    ev_tr = {}
    ev_stA = {}
    wait("tensor", "sG", ev_idb)
    for i in range(NT):
        sl = 1024 * (i % 2)
        if i >= 2:
            wait("sync", "sV", ev_app[i - 2])
        ev_x1[i] = step("sync", lambda e, i=i, sl=sl: e.dma_start(
            out=xt[:, sl:sl + 1024],
            in_=xpos[128 * i:128 * (i + 1), :]), "sL", 16)
        wait("scalar", "sL", ev_x1[i])
        if i >= 2:
            wait("scalar", "sP", ev_tr[i - 2])
        step("scalar", lambda e, i=i, sl=sl: e.activation(
            out=ln_tmp[:, sl:sl + 1024], in_=xt[:, sl:sl + 1024],
            func=AF.Identity, accum_out=m_all[:, i:i + 1]))
        ev_stA[i] = step("scalar", lambda e, i=i, sl=sl: e.activation(
            out=ln_tmp[:, sl:sl + 1024], in_=xt[:, sl:sl + 1024],
            func=AF.Square, accum_out=ssqa[:, i:i + 1]), "sA")
        wait("vector", "sA", ev_stA[i])
        step("vector", lambda e, i=i: e.tensor_scalar(
            out=m_all[:, i:i + 1], in0=m_all[:, i:i + 1], scalar1=1.0 / E,
            scalar2=None, op0=ALU.mult))
        step("vector", lambda e, i=i: e.tensor_scalar(
            out=e2a[:, i:i + 1], in0=ssqa[:, i:i + 1], scalar1=1.0 / E,
            scalar2=None, op0=ALU.mult))
        step("vector", lambda e: e.drain())
        step("vector", lambda e, i=i: e.tensor_tensor(
            out=msqa[:, i:i + 1], in0=m_all[:, i:i + 1], in1=m_all[:, i:i + 1],
            op=ALU.mult))
        step("vector", lambda e: e.drain())
        step("vector", lambda e, i=i: e.tensor_tensor(
            out=var_all[:, i:i + 1], in0=e2a[:, i:i + 1], in1=msqa[:, i:i + 1],
            op=ALU.subtract))
        step("vector", lambda e: e.drain())
        ev_vcol = step("vector", lambda e, i=i: e.tensor_scalar(
            out=var_all[:, i:i + 1], in0=var_all[:, i:i + 1], scalar1=EPS_LN,
            scalar2=None, op0=ALU.add), "sV")
        wait("scalar", "sV", ev_vcol)
        step("scalar", lambda e, i=i: e.activation(
            out=lnv[:, i:i + 1], in_=var_all[:, i:i + 1], func=AF.Ln))
        step("scalar", lambda e: e.drain())
        ev_rstd_t = step("scalar", lambda e, i=i: e.activation(
            out=rstd_all[:, i:i + 1], in_=lnv[:, i:i + 1], func=AF.Exp,
            scale=-0.5), "sA")
        wait("vector", "sA", ev_rstd_t)
        ev_app[i] = step("vector", lambda e, i=i, sl=sl: e.tensor_scalar(
            out=ln_tmp[:, sl:sl + 1024],
            in0=xt[:, sl:sl + 1024],
            scalar1=m_all[:, i:i + 1], scalar2=rstd_all[:, i:i + 1],
            op0=ALU.subtract, op1=ALU.mult), "sV")
        wait("tensor", "sV", ev_app[i])
        if i >= 1:
            wait("tensor", "sV", ev_lnfm[i - 1])
        for j in range(8):
            def trj(e, i=i, j=j, sl=sl):
                return e.transpose(
                    out=pt[:, 128 * j:128 * (j + 1)],
                    in_=ln_tmp[:, sl + 128 * j:sl + 128 * (j + 1)],
                    identity=idb[:, :])
            if j < 7:
                step("tensor", trj)
            else:
                ev_tr[i] = step("tensor", trj, "sP")
        wait("vector", "sP", ev_tr[i])
        dst = ln_fm[:, :].rearrange("p (j t) -> p j t", j=8)[:, :, 128 * i:128 * (i + 1)]
        ev_lnfm[i] = step("vector", lambda e, dst=dst: e.tensor_copy(
            out=dst, in_=pt[:, 0:1024]), "sV")

    stXT.close()
    stVd = contextlib.ExitStack()
    Vd = mk(stVd, "Vd", [128, 16384], BF16, side="right")

    pslot_ev = {0: None, 1: None}
    gidx = 0

    def ln_ready_ev(tb):
        return ev_lnfm[min(4 * tb + 3, NT - 1)]

    def emit_proj(bias_row, dst, ev_w_loaded):
        nonlocal gidx
        last_mm = None
        last_dve = None
        last_act = None
        for fc in range(8):
            for tb in range(4):
                slot = gidx % 2
                gidx += 1
                base = 1024 * slot
                wait("tensor", "sG", ev_w_loaded)
                wait("tensor", "sV", ln_ready_ev(tb))
                pe_ev = pslot_ev[slot]
                if pe_ev is not None:
                    wait("tensor", "sV", pe_ev[0])
                    wait("tensor", "sA", pe_ev[1])
                for j in range(8):
                    def mmj(e, j=j, fc=fc, tb=tb, base=base):
                        return e.matmul(
                            out=ps[:, base:base + 512],
                            lhsT=wbuf[:, 1024 * j + 128 * fc:1024 * j + 128 * (fc + 1)],
                            rhs=ln_fm[:, 2048 * j + 512 * tb:2048 * j + 512 * (tb + 1)],
                            start=(j == 0), stop=False)
                    step("tensor", mmj)
                def mmb(e, fc=fc, base=base, br=bias_row):
                    return e.matmul(
                        out=ps[:, base:base + 512],
                        lhsT=br[0:1, 128 * fc:128 * (fc + 1)],
                        rhs=onesb[0:1, 0:512],
                        start=False, stop=True)
                last_mm = step("tensor", mmb, "sP")
                wait("vector", "sP", last_mm)
                wait("scalar", "sP", last_mm)
                for k, (i_loc, gp) in enumerate(((0, 0), (0, 1), (1, 0), (1, 1))):
                    i_head = 2 * fc + i_loc
                    src = ps[:, base:base + 512].rearrange(
                        "p (mm q) -> p mm q", mm=32)[
                        64 * i_loc:64 * i_loc + 64, :, 8 * gp:8 * gp + 8]
                    dstv = dst[:, :].rearrange(
                        "p (mm i tl) -> p mm i tl", i=16, tl=8)[
                        64 * gp:64 * gp + 64, 32 * tb:32 * tb + 32,
                        i_head:i_head + 1, :]
                    if k < 2:
                        last_dve = step("vector", lambda e, dstv=dstv, src=src:
                                        e.tensor_copy(out=dstv, in_=src), "sV")
                    else:
                        last_act = step("scalar", lambda e, dstv=dstv, src=src:
                                        e.activation(out=dstv, in_=src,
                                                     func=AF.Identity), "sA")
                pslot_ev[slot] = (last_dve, last_act)
        return last_mm, last_dve, last_act

    ev_wv = None
    for j in range(8):
        ev_wv = step("gpsimd", lambda e, j=j: e.dma_start(
            out=wbuf[:, 1024 * j:1024 * (j + 1)], in_=wvT[128 * j:128 * (j + 1), :]), "sG", 16)
    wait("gpsimd", "sG", ev_bv)
    ev_vmm, ev_vdve, ev_vact = emit_proj(bvl, Vd, ev_wv)

    ev_vh_last = None
    ev_vh_tr_last = None
    prev_vh_evac = None
    for g4 in range(64):
        wait("tensor", "sV", ev_vdve)
        wait("tensor", "sA", ev_vact)
        if g4 == 0:
            wait("tensor", "sV", ev_lnfm[NT - 1])
        elif prev_vh_evac is not None:
            wait("tensor", "sA", prev_vh_evac)
        mmt = None
        for k in range(4):
            g = 4 * g4 + k
            m, gp = g // 2, g % 2
            def trv(e, m=m, gp=gp, k=k, g4=g4):
                return e.transpose(
                    out=pt[:, 1024 * gp + 512 * (g4 % 2) + 64 * (k // 2):
                           1024 * gp + 512 * (g4 % 2) + 64 * (k // 2) + 64],
                    in_=Vd[64 * gp:64 * gp + 64, 128 * m:128 * (m + 1)],
                    identity=idb[64 * gp:64 * gp + 64, 64 * gp:64 * gp + 64])
            if k < 3:
                step("tensor", trv)
            else:
                mmt = step("tensor", trv, "sP")
        ev_vh_tr_last = mmt
        wait("scalar", "sP", mmt)
        src_ap = pt[:, :].rearrange("p (gp q mg c) -> p mg gp q c", gp=2, q=2, mg=8)[
            :, 0:2, :, g4 % 2:g4 % 2 + 1, :]
        prev_vh_evac = step("scalar", lambda e, g4=g4, src_ap=src_ap: e.activation(
            out=Vh[:, :].rearrange("p (g4x mm gp c) -> p g4x mm gp c",
                                   mm=2, gp=2, c=64)[:, g4:g4 + 1, :, :, :],
            in_=src_ap, func=AF.Identity), "sA")
        ev_vh_last = prev_vh_evac

    stVd.close()
    stQK = contextlib.ExitStack()
    Qd = mk(stQK, "Qd", [128, 16384], BF16, side="right")
    Kd = mk(stQK, "Kd", [128, 16384], BF16, side="right")

    wait("gpsimd", "sP", ev_vmm)
    ev_wq = None
    for j in range(8):
        ev_wq = step("gpsimd", lambda e, j=j: e.dma_start(
            out=wbuf[:, 1024 * j:1024 * (j + 1)], in_=wqT[128 * j:128 * (j + 1), :]), "sG", 16)
    ev_qmm, _, _ = emit_proj(bql, Qd, ev_wq)

    wait("gpsimd", "sP", ev_qmm)
    ev_wk = None
    for j in range(8):
        ev_wk = step("gpsimd", lambda e, j=j: e.dma_start(
            out=wbuf[:, 1024 * j:1024 * (j + 1)], in_=wkT[128 * j:128 * (j + 1), :]), "sG", 16)
    ev_kmm, ev_kdve, ev_kact = emit_proj(bkl, Kd, ev_wk)

    expA = mk(stQK, "expA", [128, 2048], BF16, side="right")
    expAT = mk(stQK, "expAT", [128, 2048], BF16, side="right")

    wait("tensor", "sV", ev_kdve)
    wait("tensor", "sA", ev_kact)
    wait("tensor", "sA", ev_vh_last)
    wait("tensor", "sG", ev_msk)
    wait("scalar", "sV", ev_mset)

    ev_exp = {}
    ev_trx = {}
    ev_atx = {}
    ev_avout = {}
    ev_zc = {}
    ev_ts = {}
    zcv_ev = {}
    zca_ev = {}
    ev_smaskd = {}

    def pt_slot(m):
        s4 = m % 4
        return 1024 * (s4 % 2) + 512 * (s4 // 2)

    def emit_S(m):
        if m >= 2:
            wait("tensor", "sA", ev_exp[(m - 2, 3)])
        for k in range(4):
            mm = 2 * m + k // 2
            gp = k % 2
            col = 1024 * (m % 2) + 512 * gp + 128 * (k // 2)
            def mmS(e, mm=mm, gp=gp, col=col, k=k):
                return e.matmul(
                    out=ps[:, col:col + 128],
                    lhsT=Qd[64 * gp:64 * gp + 64, 128 * mm:128 * (mm + 1)],
                    rhs=Kd[64 * gp:64 * gp + 64, 128 * mm:128 * (mm + 1)],
                    start=(k < 2), stop=False)
            step("tensor", mmS)
        evm = None
        for k in range(4):
            gp = k % 2
            col = 1024 * (m % 2) + 512 * gp + 128 * (k // 2)
            def mmM(e, col=col):
                return e.matmul(
                    out=ps[:, col:col + 128],
                    lhsT=maskb[:, :], rhs=idb[:, :],
                    start=False, stop=True)
            evm = step("tensor", mmM, "sP")
        ev_smaskd[m] = evm

    def emit_exp(m):
        q4 = m % 4
        wait("scalar", "sP", ev_smaskd[m])
        if m >= 4:
            wait("scalar", "sP", ev_trx[m - 4])
        for k in range(4):
            gp = k % 2
            col = 1024 * (m % 2) + 512 * gp + 128 * (k // 2)
            def expf(e, col=col, q4=q4, k=k):
                return e.activation(
                    out=expA[:, 512 * q4 + 128 * k:512 * q4 + 128 * (k + 1)],
                    in_=ps[:, col:col + 128],
                    func=AF.Exp, scale=0.125, bias=bneg[:, 0:1],
                    accum_out=sums[:, 4 * q4 + k:4 * q4 + k + 1])
            ev_exp[(m, k)] = step("scalar", expf, "sA")

    def emit_norm(m):
        q4 = m % 4
        wait("vector", "sA", ev_exp[(m, 3)])
        step("vector", lambda e, q4=q4: e.reciprocal(
            out=recs[:, 4 * q4:4 * q4 + 4], in_=sums[:, 4 * q4:4 * q4 + 4]))
        step("vector", lambda e: e.drain())
        ev = None
        for k in range(4):
            ev = step("vector", lambda e, q4=q4, k=k: e.tensor_scalar(
                out=expA[:, 512 * q4 + 128 * k:512 * q4 + 128 * (k + 1)],
                in0=expA[:, 512 * q4 + 128 * k:512 * q4 + 128 * (k + 1)],
                scalar1=recs[:, 4 * q4 + k:4 * q4 + k + 1], scalar2=None,
                op0=ALU.mult), "sV")
        ev_ts[m] = ev

    def emit_trx(m):
        q4 = m % 4
        base = pt_slot(m)
        wait("tensor", "sV", ev_ts[m])
        if m >= 2:
            wait("tensor", "sA", ev_atx[m - 2])
        for k in range(4):
            def trx(e, base=base, q4=q4, k=k):
                return e.transpose(
                    out=pt[:, base + 128 * k:base + 128 * (k + 1)],
                    in_=expA[:, 512 * q4 + 128 * k:512 * q4 + 128 * (k + 1)],
                    identity=idb[:, :])
            if k < 3:
                step("tensor", trx)
            else:
                ev_trx[m] = step("tensor", trx, "sP")

    def emit_atx(m):
        q4 = m % 4
        base = pt_slot(m)
        wait("scalar", "sP", ev_trx[m])
        if m >= 4:
            wait("scalar", "sP", ev_avout[m - 4])
        ev_atx[m] = step("scalar", lambda e, base=base, q4=q4: e.activation(
            out=expAT[:, 512 * q4:512 * (q4 + 1)],
            in_=pt[:, base:base + 512],
            func=AF.Identity), "sA")

    def emit_av(m):
        q4 = m % 4
        wait("tensor", "sA", ev_atx[m])
        if m >= 2:
            wait("tensor", "sV", zcv_ev[m - 2])
            wait("tensor", "sA", zca_ev[m - 2])
        mmav = None
        for k in range(4):
            mm = 2 * m + k // 2
            gg = 2 * mm + (k % 2)
            def mmv(e, gg=gg, q4=q4, k=k, m=m):
                return e.matmul(
                    out=pu[0:64, 512 * (m % 2) + 128 * k:512 * (m % 2) + 128 * (k + 1)],
                    lhsT=Vh[:, 64 * gg:64 * (gg + 1)],
                    rhs=expAT[:, 512 * q4 + 128 * k:512 * q4 + 128 * (k + 1)],
                    start=True, stop=True)
            if k < 3:
                step("tensor", mmv)
            else:
                mmav = step("tensor", mmv, "sP")
        ev_avout[m] = mmav

    def zc_ap(m, pi, par):
        src = pu[:, 512 * (m % 2) + 256 * pi:512 * (m % 2) + 256 * pi + 256]\
            .rearrange("p (kg i u2 pr) -> p kg u2 pr i", kg=2, i=16, u2=4)[
            0:64, :, :, par:par + 1, :]
        dst = Zfm[:, :].rearrange(
            "p (kg u2 i a) -> p kg u2 i a", kg=2, u2=4, i=16)[
            64 * par:64 * par + 64, :, :, :, 2 * m + pi:2 * m + pi + 1]
        return src, dst

    def emit_zcv(m):
        wait("vector", "sP", ev_avout[m])
        ev = None
        for pi, par in ((0, 0), (0, 1), (1, 0)):
            src, dst = zc_ap(m, pi, par)
            ev = step("vector", lambda e, dst=dst, src=src: e.tensor_copy(
                out=dst, in_=src), "sV")
        zcv_ev[m] = ev

    def emit_zca(m):
        wait("scalar", "sP", ev_avout[m])
        wait("scalar", "sV", zcv_ev[m])
        src, dst = zc_ap(m, 1, 1)
        zca_ev[m] = step("scalar", lambda e, dst=dst, src=src: e.activation(
            out=dst, in_=src, func=AF.Identity), "sA")

    for t in range(67):
        if t <= 63:
            emit_S(t)
        if 1 <= t <= 64:
            emit_exp(t - 1)
            emit_norm(t - 1)
            emit_trx(t - 1)
        if 2 <= t <= 65:
            emit_atx(t - 2)
            emit_av(t - 2)
            emit_zcv(t - 2)
        if 3 <= t <= 66:
            emit_zca(t - 3)

    ev_explast = ev_exp[(63, 3)]
    ev_avlast = ev_avout[63]
    ev_zlast = (zcv_ev[63], zca_ev[63])
    ev_atxlast = ev_atx[63]

    stQK.close()
    stO = contextlib.ExitStack()
    woT_sb = mk(stO, "woT_sb", [128, 8192], BF16, side="right")
    xrt = mk(stO, "xrt", [128, 2048], F32, side="right")
    r1w = mk(stO, "r1w", [128, 2048], F32, side="right")
    sqw = mk(stO, "sqw", [128, 1024], F32, side="right")
    bfb = mk(es, "bfb", [128, E], F32)
    bwsx = mk(es, "bwsx", [128, E], F32)
    fsc_sb = mk(es, "fsc_sb", [1, E], F32)
    fb_sb = mk(es, "fb_sb", [1, E], F32)
    wsxs = mk(es, "wsxs", [1, E], F32)
    wait("gpsimd", "sP", ev_avlast)
    ev_fsc = step("gpsimd", lambda e: e.dma_start(out=fsc_sb[:, :], in_=fscr[:, :]), "sG", 16)
    ev_fb = step("gpsimd", lambda e: e.dma_start(out=fb_sb[:, :], in_=fbr[:, :]), "sG", 16)
    ev_wo = None
    for j in range(8):
        ev_wo = step("gpsimd", lambda e, j=j: e.dma_start(
            out=woT_sb[:, 1024 * j:1024 * (j + 1)], in_=woT[128 * j:128 * (j + 1), :]), "sG", 16)

    ev_pb_free = None

    def emit_bcast_row(src_row, dst_fm, ev_src_g, ev_src_v=None):
        nonlocal ev_pb_free
        out_ev = None
        for h in range(2):
            wait("tensor", "sG", ev_src_g)
            wait("tensor", "sV", ev_src_v)
            if ev_pb_free is not None:
                wait("tensor", "sV", ev_pb_free)
            def mmh(e, h=h, src_row=src_row):
                return e.matmul(out=pb[:, :], lhsT=onesf[0:1, 0:128],
                                rhs=src_row[0:1, 512 * h:512 * (h + 1)],
                                start=True, stop=True)
            mm = step("tensor", mmh, "sP")
            wait("vector", "sP", mm)
            ev_pb_free = step("vector", lambda e, h=h, dst_fm=dst_fm: e.tensor_copy(
                out=dst_fm[:, 512 * h:512 * (h + 1)], in_=pb[:, :]), "sV")
            out_ev = ev_pb_free
        return out_ev

    wait("tensor", "sA", ev_explast)
    emit_bcast_row(fb_sb, bfb, ev_fb)

    ev_p1 = {}
    ev_ttl = {}
    ev_r1dma = {}
    oslot_ev = {0: None, 1: None}
    lastmm_hist = {}
    wait("sync", "sP", ev_avlast)
    for sb in range(16):
        sl = 1024 * (sb % 2)
        if sb >= 2:
            wait("sync", "sP", lastmm_hist[sb - 2])
        ev_xr = step("sync", lambda e, sb=sb, sl=sl: e.dma_start(
            out=xrt[:, sl:sl + 1024], in_=xres[128 * sb:128 * (sb + 1), :]), "sL", 16)
        lastmm = {}
        for half in range(2):
            base = 1024 * half
            wait("tensor", "sG", ev_wo)
            wait("tensor", "sV", ev_zlast[0])
            wait("tensor", "sA", ev_zlast[1])
            if sb == 0 and half == 0:
                wait("tensor", "sA", ev_explast)
            if oslot_ev[half] is not None:
                wait("tensor", "sA", oslot_ev[half])
            if sb == 0 and half == 0:
                wait("tensor", "sG", ev_bo2)
                wait("tensor", "sG", ev_idf)
            for c in range(8):
                def mmo(e, c=c, sb=sb, half=half, base=base):
                    return e.matmul(
                        out=ps[:, base:base + 512],
                        lhsT=Zfm[:, 2048 * c + 128 * sb:2048 * c + 128 * (sb + 1)],
                        rhs=woT_sb[:, 1024 * c + 512 * half:1024 * c + 512 * (half + 1)],
                        start=(c == 0), stop=False)
                step("tensor", mmo)
            def mmob(e, half=half, base=base):
                return e.matmul(
                    out=ps[:, base:base + 512],
                    lhsT=onesb[0:1, 0:128],
                    rhs=bol[0:1, 512 * half:512 * (half + 1)],
                    start=False, stop=False)
            step("tensor", mmob)
            wait("tensor", "sL", ev_xr)
            def mmxr(e, half=half, base=base, sl=sl):
                return e.matmul(
                    out=ps[:, base:base + 512],
                    lhsT=idf[:, :],
                    rhs=xrt[:, sl + 512 * half:sl + 512 * (half + 1)],
                    start=False, stop=True)
            lastmm[half] = step("tensor", mmxr, "sP")
        if sb >= 2:
            wait("scalar", "sG", ev_r1dma[sb - 2])
        if sb >= 1:
            wait("scalar", "sV", ev_p1[sb - 1])
        for half in range(2):
            wait("scalar", "sP", lastmm[half])
            oslot_ev[half] = step("scalar", lambda e, sb=sb, half=half, sl=sl: e.activation(
                out=r1w[:, sl + 512 * half:sl + 512 * (half + 1)],
                in_=ps[:, 1024 * half:1024 * half + 512],
                func=AF.Identity, accum_out=m2h[:, 2 * sb + half:2 * sb + half + 1]), "sA")
        ev_ttl[sb] = oslot_ev[1]
        lastmm_hist[sb] = lastmm[1]
        ev_sq = step("scalar", lambda e, sb=sb, sl=sl: e.activation(
            out=sqw[:, :], in_=r1w[:, sl:sl + 1024],
            func=AF.Square, accum_out=ssqa[:, sb:sb + 1]), "sA")
        wait("vector", "sA", ev_sq)
        step("vector", lambda e, sb=sb: e.tensor_tensor(
            out=m2[:, sb:sb + 1], in0=m2h[:, 2 * sb:2 * sb + 1],
            in1=m2h[:, 2 * sb + 1:2 * sb + 2], op=ALU.add))
        step("vector", lambda e: e.drain())
        step("vector", lambda e, sb=sb: e.tensor_scalar(
            out=m2[:, sb:sb + 1], in0=m2[:, sb:sb + 1], scalar1=1.0 / E, scalar2=None,
            op0=ALU.mult))
        step("vector", lambda e: e.drain())
        ev_nm = step("vector", lambda e, sb=sb: e.tensor_scalar(
            out=nm2[:, sb:sb + 1], in0=m2[:, sb:sb + 1], scalar1=-1.0, scalar2=None,
            op0=ALU.mult), "sV")
        wait("scalar", "sV", ev_nm)
        ev_abs = step("scalar", lambda e, sb=sb, sl=sl: e.activation(
            out=sqw[:, :], in_=r1w[:, sl:sl + 1024],
            func=AF.Abs, bias=nm2[:, sb:sb + 1]), "sA")
        wait("vector", "sA", ev_abs)
        ev_p1[sb] = step("vector", lambda e, sb=sb: e.reduce_max(
            out=cmax[:, sb:sb + 1], in_=sqw[:, :], axis=AX), "sV")
        wait("gpsimd", "sA", ev_ttl[sb])
        ev_r1dma[sb] = step("gpsimd", lambda e, sb=sb, sl=sl: e.dma_start(
            out=r1d[128 * sb:128 * (sb + 1), :], in_=r1w[:, sl:sl + 1024]), "sG", 16)

    step("vector", lambda e: e.drain())
    step("vector", lambda e: e.tensor_scalar(
        out=e2a[:, :], in0=ssqa[:, :], scalar1=1.0 / E, scalar2=None, op0=ALU.mult))
    step("vector", lambda e: e.tensor_tensor(
        out=msqa[:, :], in0=m2[:, :], in1=m2[:, :], op=ALU.mult))
    step("vector", lambda e: e.drain())
    step("vector", lambda e: e.tensor_tensor(
        out=var2[:, :], in0=e2a[:, :], in1=msqa[:, :], op=ALU.subtract))
    step("vector", lambda e: e.drain())
    ev_v2 = step("vector", lambda e: e.tensor_scalar(
        out=var2[:, :], in0=var2[:, :], scalar1=EPS_LN, scalar2=None, op0=ALU.add), "sV")
    wait("scalar", "sV", ev_v2)
    step("scalar", lambda e: e.activation(out=lnv2[:, :], in_=var2[:, :], func=AF.Ln))
    step("scalar", lambda e: e.drain())
    ev_rstd2 = step("scalar", lambda e: e.activation(
        out=rstd2[:, :], in_=lnv2[:, :], func=AF.Exp, scale=-0.5), "sA")
    wait("vector", "sA", ev_rstd2)
    step("vector", lambda e: e.tensor_tensor(
        out=am16[:, :], in0=cmax[:, :], in1=rstd2[:, :], op=ALU.mult))
    step("vector", lambda e: e.drain())
    ev_amc = step("vector", lambda e: e.reduce_max(
        out=amcol[:, :], in_=am16[:, :], axis=AX), "sV")
    wait("tensor", "sV", ev_amc)
    wait("tensor", "sG", ev_idf)
    ev_amt = step("tensor", lambda e: e.transpose(
        out=pu[0:1, 0:128], in_=amcol[:, :], identity=idf[:, :]), "sP")
    wait("vector", "sP", ev_amt)
    ev_am1 = step("vector", lambda e: e.reduce_max(
        out=am_row[0:1, 0:1], in_=pu[0:1, 0:128], axis=AX), "sV")

    wait("gpsimd", "sV", ev_am1)
    ev_arin = step("gpsimd", lambda e: e.dma_start(out=arin[:, :], in_=am_row[:, :]), "sG", 16)
    wait("gpsimd", "sG", ev_arin)
    ev_ar = step("gpsimd", lambda e: e.collective_compute(
        "AllReduce", ALU.max, replica_groups=ALL8,
        ins=[arin.opt()], outs=[arout.opt()]), "sCC", 1)
    wait("gpsimd", "sCC", ev_ar)
    ev_xs = step("gpsimd", lambda e: e.dma_start(out=xs_sb[:, :], in_=arout[:, :]), "sG", 16)

    wait("vector", "sG", ev_xs)
    step("vector", lambda e: e.tensor_scalar(
        out=xsv[:, :], in0=xs_sb[0:1, 0:1], scalar1=1.0 / 127.0, scalar2=EPS_Q,
        op0=ALU.mult, op1=ALU.max))
    step("vector", lambda e: e.drain())
    step("vector", lambda e: e.reciprocal(out=xsinv[:, :], in_=xsv[:, :]))
    step("vector", lambda e: e.drain())
    wait("vector", "sG", ev_fsc)
    ev_wsxs = step("vector", lambda e: e.tensor_scalar(
        out=wsxs[:, :], in0=fsc_sb[:, :], scalar1=xsv[0:1, 0:1], scalar2=None,
        op0=ALU.mult), "sV")
    wait("tensor", "sV", ev_wsxs)
    ev_xc = step("tensor", lambda e: e.matmul(
        out=pb[:, 0:1], lhsT=onesf[0:1, 0:128], rhs=xsinv[0:1, 0:1],
        start=True, stop=True), "sP")
    wait("vector", "sP", ev_xc)
    ev_xcc = step("vector", lambda e: e.tensor_copy(out=xs_col[:, :], in_=pb[:, 0:1]), "sV")
    ev_pb_free = ev_xcc
    ev_bwsx = emit_bcast_row(wsxs, bwsx, None, ev_src_v=ev_wsxs)
    wait("vector", "sV", ev_xcc)
    step("vector", lambda e: e.drain())
    ev_rsxs = step("vector", lambda e: e.tensor_scalar(
        out=rsxs[:, :], in0=rstd2[:, :], scalar1=xs_col[:, :], scalar2=None,
        op0=ALU.mult), "sV")

    stO.close()
    stF = contextlib.ExitStack()
    w8s = mk(stF, "w8s", [128, 8192], I8, side="right")
    wf = mk(stF, "wf", [128, 8192], BF16, side="right")
    r1t = mk(stF, "r1t", [128, 2048], F32, side="right")
    qzb = mk(stF, "qzb", [128, 1024], BF16, side="right")
    xq = mk(stF, "xq", [128, 1024], BF16, side="right")
    ysb = mk(stF, "ysb", [128, 2048], F32, side="right")
    es.enter_context(stF)

    wait("gpsimd", "sP", ev_amt)
    ev_w8 = None
    for j in range(8):
        ev_w8 = step("gpsimd", lambda e, j=j: e.dma_start(
            out=w8s[:, 1024 * j:1024 * (j + 1)], in_=w8T[128 * j:128 * (j + 1), :]), "sG", 16)

    wait("vector", "sG", ev_w8)
    wait("vector", "sG", ev_r1dma[NOT - 1])
    wait("vector", "sV", ev_rsxs)
    ev_wf = None
    for j in range(8):
        ev_wf = step("vector", lambda e, j=j: e.tensor_copy(
            out=wf[:, 1024 * j:1024 * (j + 1)], in_=w8s[:, 1024 * j:1024 * (j + 1)]), "sV")

    ev_ydma = {}
    ev_ffnev = {}
    ev_pt_free2 = None
    ffnslot_ev = {0: None, 1: None}
    ev_xq_cons = None
    for ot in range(NOT):
        sl = 1024 * (ot % 2)
        if ot >= 2:
            wait("sync", "sV", ev_ffnev[ot - 2])
        else:
            wait("sync", "sG", ev_xs)
        wait("sync", "sG", ev_r1dma[ot])
        ev_l = step("sync", lambda e, ot=ot, sl=sl: e.dma_start(
            out=r1t[:, sl:sl + 1024], in_=r1d[128 * ot:128 * (ot + 1), :]), "sL", 16)
        wait("vector", "sL", ev_l)
        if ot >= 2:
            wait("vector", "sY", ev_ydma[ot - 2])
        step("vector", lambda e, ot=ot, sl=sl: e.tensor_scalar(
            out=ysb[:, sl:sl + 1024], in0=r1t[:, sl:sl + 1024],
            scalar1=m2[:, ot:ot + 1], scalar2=rsxs[:, ot:ot + 1],
            op0=ALU.subtract, op1=ALU.mult))
        step("vector", lambda e, sl=sl: e.tensor_scalar(
            out=ysb[:, sl:sl + 1024], in0=ysb[:, sl:sl + 1024], scalar1=MAGIC, scalar2=MAGIC,
            op0=ALU.add, op1=ALU.subtract))
        ev_q = step("vector", lambda e, sl=sl: e.tensor_scalar(
            out=qzb[:, :], in0=ysb[:, sl:sl + 1024], scalar1=127.0, scalar2=-128.0,
            op0=ALU.min, op1=ALU.max), "sV")
        wait("tensor", "sV", ev_q)
        if ev_pt_free2 is None:
            wait("tensor", "sA", ev_atxlast)
        else:
            wait("tensor", "sV", ev_pt_free2)
        mmt = None
        for j in range(8):
            def trj(e, j=j):
                return e.transpose(
                    out=pt[:, 128 * j:128 * (j + 1)],
                    in_=qzb[:, 128 * j:128 * (j + 1)],
                    identity=idb[:, :])
            if j < 7:
                step("tensor", trj)
            else:
                mmt = step("tensor", trj, "sP")
        wait("vector", "sP", mmt)
        if ev_xq_cons is not None:
            wait("vector", "sP", ev_xq_cons)
        ev_xqc = step("vector", lambda e: e.tensor_copy(out=xq[:, :], in_=pt[:, 0:1024]), "sV")
        ev_pt_free2 = ev_xqc
        wait("tensor", "sV", ev_xqc)
        wait("tensor", "sV", ev_wf)
        lastmm = None
        for half in range(2):
            base = 1024 * half
            if ffnslot_ev[half] is not None:
                wait("tensor", "sV", ffnslot_ev[half])
            for j in range(8):
                def mmf(e, j=j, half=half, base=base):
                    return e.matmul(
                        out=ps[:, base:base + 512],
                        lhsT=xq[:, 128 * j:128 * (j + 1)],
                        rhs=wf[:, 1024 * j + 512 * half:1024 * j + 512 * (half + 1)],
                        start=(j == 0), stop=(j == 7))
                if j < 7:
                    step("tensor", mmf)
                else:
                    lastmm = step("tensor", mmf, "sP")
            wait("vector", "sP", lastmm)
            if half == 0:
                wait("vector", "sV", ev_bwsx)
            step("vector", lambda e, half=half, base=base, sl=sl: e.tensor_tensor(
                out=ysb[:, sl + 512 * half:sl + 512 * (half + 1)],
                in0=ps[:, base:base + 512],
                in1=bwsx[:, 512 * half:512 * (half + 1)], op=ALU.mult))
            step("vector", lambda e, half=half, sl=sl: e.tensor_tensor(
                out=ysb[:, sl + 512 * half:sl + 512 * (half + 1)],
                in0=ysb[:, sl + 512 * half:sl + 512 * (half + 1)],
                in1=bfb[:, 512 * half:512 * (half + 1)], op=ALU.add))
            ev = step("vector", lambda e, half=half, sl=sl: e.tensor_tensor(
                out=ysb[:, sl + 512 * half:sl + 512 * (half + 1)],
                in0=ysb[:, sl + 512 * half:sl + 512 * (half + 1)],
                in1=r1t[:, sl + 512 * half:sl + 512 * (half + 1)],
                op=ALU.add), "sV")
            ffnslot_ev[half] = ev
            ev_ffnev[ot] = ev
        ev_xq_cons = lastmm
        wait("gpsimd", "sV", ev_ffnev[ot])
        ev_ydma[ot] = step("gpsimd", lambda e, ot=ot, sl=sl: e.dma_start(
            out=y[128 * ot:128 * (ot + 1), :], in_=ysb[:, sl:sl + 1024]), "sY", 16)

    wait("gpsimd", "sY", ev_ydma[NOT - 1])

    @block.sync
    def _(e):
        for f in P["sync"]:
            f(e)

    @block.gpsimd
    def _(e):
        for f in P["gpsimd"]:
            f(e)

    @block.vector
    def _(e):
        for f in P["vector"]:
            f(e)

    @block.scalar
    def _(e):
        for f in P["scalar"]:
            f(e)

    @block.tensor
    def _(e):
        for f in P["tensor"]:
            f(e)

    return nc, es


def _mask_block():
    m = np.full((128, 128), MASKV, dtype=np.float32)
    r = np.arange(128)
    m[(r[:, None] % 8) == (r[None, :] % 8)] = 0.0
    return m


def kernel(L0_attn_x, L0_attn_Wq, L0_attn_bq, L0_attn_Wk, L0_attn_bk,
           L0_attn_Wv, L0_attn_bv, L0_attn_Wo, L0_attn_bo,
           L0_ffn_weight_int8, L0_ffn_weight_scale, L0_ffn_bias):
    X = np.asarray(L0_attn_x, dtype=np.float32)
    Wq = np.asarray(L0_attn_Wq, dtype=np.float32)
    Wk = np.asarray(L0_attn_Wk, dtype=np.float32)
    Wv = np.asarray(L0_attn_Wv, dtype=np.float32)
    Wo = np.asarray(L0_attn_Wo, dtype=np.float32)
    bq = np.asarray(L0_attn_bq, dtype=np.float32).reshape(1, E)
    bk = np.asarray(L0_attn_bk, dtype=np.float32).reshape(1, E)
    bv = np.asarray(L0_attn_bv, dtype=np.float32).reshape(1, E)
    bo = np.asarray(L0_attn_bo, dtype=np.float32).reshape(1, E)
    W8 = np.asarray(L0_ffn_weight_int8)
    fsc = np.asarray(L0_ffn_weight_scale, dtype=np.float32).reshape(1, E)
    fb = np.asarray(L0_ffn_bias, dtype=np.float32).reshape(1, E)

    wqTh = np.ascontiguousarray(Wq.T)
    wkTh = np.ascontiguousarray(Wk.T)
    wvTh = np.ascontiguousarray(Wv.T)
    woTh = np.ascontiguousarray(Wo.T)
    w8Th = np.ascontiguousarray(W8.T)
    ident = np.eye(128, dtype=np.float32)
    maskc = _mask_block()

    in_maps = []
    for c in range(8):
        n, hh = c // 2, c % 2
        xn = np.ascontiguousarray(X[n])
        in_maps.append({
            "xpos": np.ascontiguousarray(xn[NPOS * hh:NPOS * (hh + 1)]),
            "xres": np.ascontiguousarray(
                xn.reshape(16, 2, 128, E)[:, hh].reshape(OWN, E)),
            "wqT": wqTh, "wkT": wkTh, "wvT": wvTh, "woT": woTh, "w8T": w8Th,
            "bqr": bq, "bkr": bk, "bvr": bv, "bor": bo,
            "fscr": fsc, "fbr": fb,
            "idfd": ident, "maskd": maskc,
        })

    nc, es = build()
    try:
        res = run_bass_kernel_spmd(nc, in_maps, core_ids=list(range(8)))
    finally:
        es.close()

    out = np.empty((4, 4096, E), dtype=np.float32)
    for c in range(8):
        n, hh = c // 2, c % 2
        yv = np.asarray(res.results[c]["y"]).reshape(16, 128, E)
        out[n].reshape(16, 2, 128, E)[:, hh] = yv
    return out
